# version 01
# baseline (speedup 1.0000x reference)
import numpy as np
import ml_dtypes
import concourse.bass as bass
import concourse.mybir as mybir

F32 = mybir.dt.float32
BF16 = mybir.dt.bfloat16
AF = mybir.ActivationFunctionType
ALU = mybir.AluOpType

D = 1024
NCT = 8
TQ = 16
TS = 32
OWN = 2048
FULL = 4096

bf = ml_dtypes.bfloat16


class CE:

    def __init__(self, eng, sem):
        self.e, self.s, self.n = eng, sem, 0

    def wait_ge(self, sem, v):
        self.e.wait_ge(sem, v)

    def mark(self, sem, val=1):
        inst = self.e.nop()
        if self.n > 0:
            inst._wait_ge(self.s, self.n)
        inst.then_inc(sem, val)
        return inst

    def __getattr__(self, name):
        fn = getattr(self.e, name)

        def call(*a, **k):
            inst = fn(*a, **k)
            if self.n > 0:
                inst._wait_ge(self.s, self.n)
            self.n += 1
            inst.then_inc(self.s, 1)
            return inst

        return call



def consts_np():
    ident = np.eye(128, dtype=bf)
    Rm = (np.arange(128)[:, None] - np.arange(128)[None, :]).astype(np.float32)
    Dm = np.broadcast_to(128.0 * np.arange(32)[None, :] - 1.0, (128, 32)).astype(np.float32).copy()
    ones_row = np.ones((1, 128), dtype=np.float32)
    return ident, Rm, Dm, ones_row


def pairs_list():
    out = []
    for q in range(TQ):
        sl = list(range(q, TS))
        for j, s in enumerate(sl):
            out.append((q, j, s, j == 0, j == len(sl) - 1))
    return out


def build_launch1():
    nc = bass.Bass("TRN2", target_bir_lowering=False)
    x_full = nc.dram_tensor("x_full", [FULL, D], F32, kind="ExternalInput").ap()
    qs_cm = nc.dram_tensor("qs_cm", [128, NCT], F32, kind="ExternalInput").ap()
    ks_cm = nc.dram_tensor("ks_cm", [128, NCT], F32, kind="ExternalInput").ap()
    scal = nc.dram_tensor("scal", [1, 4], F32, kind="ExternalInput").ap()
    projT = nc.dram_tensor("projT", [128, NCT * 8], F32, kind="ExternalInput").ap()
    x1_out = nc.dram_tensor("x1_out", [OWN, D], F32, kind="ExternalOutput").ap()
    z1_out = nc.dram_tensor("z1_out", [OWN, 8], F32, kind="ExternalOutput").ap()

    ident_np, Rm_np, Dm_np, ones_np = consts_np()
    identb = nc.inline_tensor(ident_np, "identb").ap()
    RmT = nc.inline_tensor(Rm_np, "Rm").ap()
    DmT = nc.inline_tensor(Dm_np, "Dm").ap()
    onesT = nc.inline_tensor(ones_np, "ones_row").ap()

    PAIRS = pairs_list()
    NPAIR = len(PAIRS)
    first_pair_idx = {}
    last_pair_idx = {}
    for idx, (q, j, s, fi, la) in enumerate(PAIRS):
        if fi:
            first_pair_idx[q] = idx
        if la:
            last_pair_idx[q] = idx

    from contextlib import ExitStack
    with ExitStack() as ctx:
        E = ctx.enter_context
        V = E(nc.sbuf_tensor("V", [128, TS * D], BF16))
        K = E(nc.sbuf_tensor("K", [128, NCT * FULL], BF16))
        Qt = E(nc.sbuf_tensor("Qt", [128, NCT * 128], BF16))
        Wt = E(nc.sbuf_tensor("Wt", [128, 16 * 128], BF16))
        Wg = E(nc.sbuf_tensor("Wg", [128, 32 * 128], BF16))
        xbuf = E(nc.sbuf_tensor("xbuf", [128, 2 * D], F32))
        identt = E(nc.sbuf_tensor("identt", [128, 128], BF16))
        Rm_s = E(nc.sbuf_tensor("Rm_s", [128, 128], F32))
        Dm_s = E(nc.sbuf_tensor("Dm_s", [128, 32], F32))
        Rl = E(nc.sbuf_tensor("Rl", [128, 128], F32))
        Dl = E(nc.sbuf_tensor("Dl", [128, 32], F32))
        ones_s = E(nc.sbuf_tensor("ones_s", [1, 128], F32))
        scal_s = E(nc.sbuf_tensor("scal_s", [1, 4], F32))
        scalc = E(nc.sbuf_tensor("scalc", [128, 4], F32))
        lnd = E(nc.sbuf_tensor("lnd", [128, 1], F32))
        sigd = E(nc.sbuf_tensor("sigd", [128, 1], F32))
        qs_s = E(nc.sbuf_tensor("qs_s", [128, NCT], F32))
        ks_s = E(nc.sbuf_tensor("ks_s", [128, NCT], F32))
        projT_s = E(nc.sbuf_tensor("projT_s", [128, NCT * 8], BF16))
        stats = E(nc.sbuf_tensor("stats", [128, 2 * 12], F32))
        mv = E(nc.sbuf_tensor("mv", [128, 2 * 2], F32))
        veps = E(nc.sbuf_tensor("veps", [128, 2], F32))
        rstd = E(nc.sbuf_tensor("rstd", [128, 2], F32))
        wtmp = E(nc.sbuf_tensor("wtmp", [128, 128], F32))
        getmp = E(nc.sbuf_tensor("getmp", [128, 128], F32))
        sw = E(nc.sbuf_tensor("sw", [128, 2 * 128], BF16))
        x1t = E(nc.sbuf_tensor("x1t", [128, 2 * D], F32))
        lnz = E(nc.sbuf_tensor("lnz", [128, D], BF16))
        lnzT = E(nc.sbuf_tensor("lnzT", [128, NCT * 128], BF16))
        z_sb = E(nc.sbuf_tensor("z_sb", [128, TQ * 8], F32))

        psA0 = E(nc.psum_tensor("psA0", [128, 512], F32))
        psA1 = E(nc.psum_tensor("psA1", [128, 512], F32))
        psS0 = E(nc.psum_tensor("psS0", [128, 512], F32))
        psS1 = E(nc.psum_tensor("psS1", [128, 512], F32))
        psT0 = E(nc.psum_tensor("psT0", [128, 1024], BF16))
        psT1 = E(nc.psum_tensor("psT1", [128, 1024], BF16))
        psZ = E(nc.psum_tensor("psZ", [128, 512], F32))
        psB = E(nc.psum_tensor("psB", [128, 512], F32))

        s_ld0 = E(nc.semaphore("s_ld0"))
        s_ld1 = E(nc.semaphore("s_ld1"))
        s_pre = E(nc.semaphore("s_pre"))
        s_out0 = E(nc.semaphore("s_out0"))
        s_out1 = E(nc.semaphore("s_out1"))
        s_outz = E(nc.semaphore("s_outz"))
        vA_st = E(nc.semaphore("vA_st"))
        sA_rs = E(nc.semaphore("sA_rs"))
        vA_nrm = E(nc.semaphore("vA_nrm"))
        tA_tr = E(nc.semaphore("tA_tr"))
        vA_ev = E(nc.semaphore("vA_ev"))
        sP_w = E(nc.semaphore("sP_w"))
        vP_w = E(nc.semaphore("vP_w"))
        tB_s = E(nc.semaphore("tB_s"))
        vB_w = E(nc.semaphore("vB_w"))
        tB_m2 = E(nc.semaphore("tB_m2"))
        vB_x1 = E(nc.semaphore("vB_x1"))
        vB_ln = E(nc.semaphore("vB_ln"))
        vB_lt = E(nc.semaphore("vB_lt"))
        tB_z = E(nc.semaphore("tB_z"))
        vB_z = E(nc.semaphore("vB_z"))
        chV = E(nc.semaphore("chV"))
        chS = E(nc.semaphore("chS"))
        block = E(nc.Block())

        def kkQ(q, ct):
            return TS * NCT + q * 16 + ct

        def kkL(q, ct):
            return TS * NCT + q * 16 + 8 + ct

        NTR = TS * NCT + TQ * 16

        @block.sync
        def _(sync):
            for i in range(TS):
                if i >= 2:
                    sync.wait_ge(vA_nrm, i - 1)
                sld = s_ld0 if i % 2 == 0 else s_ld1
                sync.dma_start(out=xbuf[:, (i % 2) * D:(i % 2 + 1) * D],
                               in_=x_full[128 * i:128 * (i + 1), :]).then_inc(sld, 16)
            for q in range(TQ):
                if q < 2:
                    sync.wait_ge(vA_nrm, TS)
                else:
                    sync.wait_ge(vB_x1, q - 1)
                sld = s_ld0 if q % 2 == 0 else s_ld1
                sync.dma_start(out=xbuf[:, (q % 2) * D:(q % 2 + 1) * D],
                               in_=x_full[128 * q:128 * (q + 1), :]).then_inc(sld, 16)

        @block.gpsimd
        def _(gpsimd):
            gpsimd.dma_start(out=identt[:], in_=identb).then_inc(s_pre, 16)
            gpsimd.dma_start(out=Rm_s[:], in_=RmT).then_inc(s_pre, 16)
            gpsimd.dma_start(out=Dm_s[:], in_=DmT).then_inc(s_pre, 16)
            gpsimd.dma_start(out=ones_s[:], in_=onesT).then_inc(s_pre, 16)
            gpsimd.dma_start(out=scal_s[:], in_=scal).then_inc(s_pre, 16)
            gpsimd.dma_start(out=qs_s[:], in_=qs_cm).then_inc(s_pre, 16)
            gpsimd.dma_start(out=ks_s[:], in_=ks_cm).then_inc(s_pre, 16)
            gpsimd.dma_start(out=projT_s[:], in_=projT).then_inc(s_pre, 16)
            for q in range(TQ):
                gpsimd.wait_ge(vB_x1, q + 1)
                so = s_out0 if q % 2 == 0 else s_out1
                gpsimd.dma_start(out=x1_out[128 * q:128 * (q + 1), :],
                                 in_=x1t[:, (q % 2) * D:(q % 2 + 1) * D]).then_inc(so, 16)
            gpsimd.wait_ge(vB_z, TQ)
            gpsimd.dma_start(out=z1_out.rearrange("(q p) e -> p q e", p=128),
                             in_=z_sb[:].rearrange("p (q e) -> p q e", e=8)).then_inc(s_outz, 16)
            gpsimd.wait_ge(s_out0, 16 * (TQ // 2))
            gpsimd.wait_ge(s_out1, 16 * (TQ // 2))
            gpsimd.wait_ge(s_outz, 16)

        def _mm2(tensor, idx):
            (q, j, s, fi, la) = PAIRS[idx]
            tensor.wait_ge(vB_w, idx + 1)
            if fi and q > 0:
                tensor.wait_ge(vB_x1, q)
            swk = sw[:, (idx % 2) * 128:(idx % 2 + 1) * 128]
            tensor.matmul(psA0[:], swk, V[:, D * s: D * s + 512],
                          start=fi, stop=la)
            tensor.matmul(psA1[:], swk, V[:, D * s + 512: D * s + 1024],
                          start=fi, stop=la).then_inc(tB_m2, 1)

        def _zwork(tensor, q):
            tensor.wait_ge(vB_ln, q + 1)
            for ct in range(NCT):
                k = kkL(q, ct)
                tensor.wait_ge(vA_ev, k - 1)
                ps = psT0 if k % 2 == 0 else psT1
                tensor.transpose(ps[:, 0:128], lnz[:, 128 * ct:128 * (ct + 1)],
                                 identt[:]).then_inc(tA_tr, 1)
            tensor.wait_ge(vB_lt, q + 1)
            if q > 0:
                tensor.wait_ge(vB_z, q)
            for ct in range(NCT):
                mi = tensor.matmul(psZ[:, 0:8], lnzT[:, 128 * ct:128 * (ct + 1)],
                                   projT_s[:, 8 * ct:8 * (ct + 1)],
                                   start=(ct == 0), stop=(ct == NCT - 1))
                if ct == NCT - 1:
                    mi.then_inc(tB_z, 1)

        @block.tensor
        def _(tensor):
            tensor.wait_ge(s_pre, 16 * 8)
            tensor.matmul(psB[:, 0:4], ones_s[:], scal_s[:], start=True, stop=True).then_inc(tA_tr, 1)
            for i in range(TS):
                tensor.wait_ge(vA_nrm, i + 1)
                for ct in range(NCT):
                    k = i * NCT + ct
                    if k >= 2:
                        tensor.wait_ge(vA_ev, k - 1)
                    ps = psT0 if k % 2 == 0 else psT1
                    tensor.transpose(ps[:, 0:128], V[:, D * i + 128 * ct: D * i + 128 * (ct + 1)],
                                     identt[:]).then_inc(tA_tr, 1)
            for q in range(TQ):
                if q >= 1:
                    _zwork(tensor, q - 1)
                for ct in range(NCT):
                    k = kkQ(q, ct)
                    tensor.wait_ge(vA_ev, k - 1)
                    ps = psT0 if k % 2 == 0 else psT1
                    tensor.transpose(ps[:, 0:128], V[:, D * q + 128 * ct: D * q + 128 * (ct + 1)],
                                     identt[:]).then_inc(tA_tr, 1)
                for idx in range(first_pair_idx[q], last_pair_idx[q] + 1):
                    (qq, j, s, fi, la) = PAIRS[idx]
                    if fi:
                        tensor.wait_ge(vA_ev, kkQ(q, NCT - 1) + 1)
                    if idx >= 2:
                        tensor.wait_ge(vB_w, idx - 1)
                    psS = psS0 if idx % 2 == 0 else psS1
                    for ct in range(NCT):
                        mi = tensor.matmul(psS[:, 0:128],
                                           K[:, FULL * ct + 128 * s: FULL * ct + 128 * (s + 1)],
                                           Qt[:, 128 * ct:128 * (ct + 1)],
                                           start=(ct == 0), stop=(ct == NCT - 1))
                        if ct == NCT - 1:
                            mi.then_inc(tB_s, 1)
                    if idx > first_pair_idx[q]:
                        _mm2(tensor, idx - 1)
                _mm2(tensor, last_pair_idx[q])
            _zwork(tensor, TQ - 1)

        @block.scalar
        def _(scalar_raw):
            sc = CE(scalar_raw, chS)
            sc.wait_ge(tA_tr, 1)
            sc.activation(out=sigd[:], in_=psB[:, 0:1], func=AF.Sigmoid)
            sc.mark(sA_rs)
            sc.wait_ge(vP_w, 1)
            sc.activation(out=lnd[:], in_=getmp[:, 0:1], func=AF.Ln)
            sc.mark(sA_rs)
            for m in range(48):
                sc.wait_ge(vP_w, m + 2)
                dst = Wt[:, 128 * m:128 * (m + 1)] if m < 16 else \
                    Wg[:, 128 * (m - 16):128 * (m - 15)]
                sc.activation(out=dst, in_=wtmp[:], func=AF.Exp)
                sc.mark(sP_w)
            for i in range(TS):
                sc.wait_ge(vA_st, i + 1)
                if i >= 2:
                    sc.wait_ge(vA_nrm, i - 1)
                sc.activation(out=rstd[:, i % 2: i % 2 + 1], in_=veps[:, i % 2:i % 2 + 1],
                              func=AF.Ln)
                sc.activation(out=rstd[:, i % 2: i % 2 + 1], in_=rstd[:, i % 2:i % 2 + 1],
                              func=AF.Exp, scale=-0.5)
                sc.mark(sA_rs)
            for q in range(TQ):
                sc.wait_ge(vA_st, TS + q + 1)
                if q >= 2:
                    sc.wait_ge(vB_ln, q - 1)
                sc.activation(out=rstd[:, q % 2: q % 2 + 1], in_=veps[:, q % 2:q % 2 + 1],
                              func=AF.Ln)
                sc.activation(out=rstd[:, q % 2: q % 2 + 1], in_=rstd[:, q % 2:q % 2 + 1],
                              func=AF.Exp, scale=-0.5)
                sc.mark(sA_rs)

        @block.vector
        def _(vector_raw):
            vc = CE(vector_raw, chV)
            vc.wait_ge(tA_tr, 1)
            vc.tensor_copy(out=scalc[:], in_=psB[:, 0:4])
            vc.wait_ge(sA_rs, 1)
            vc.tensor_scalar(out=getmp[:, 0:1], in0=sigd[:], scalar1=1e-38, scalar2=None,
                             op0=ALU.max)
            vc.mark(vP_w)
            vc.wait_ge(sA_rs, 2)
            vc.wait_ge(s_pre, 16 * 8)
            vc.tensor_scalar(out=Rl[:], in0=Rm_s[:], scalar1=lnd[:], scalar2=None, op0=ALU.mult)
            vc.tensor_scalar(out=Dl[:], in0=Dm_s[:], scalar1=lnd[:], scalar2=None, op0=ALU.mult)
            for m in range(48):
                mm = m if m < 16 else m - 16
                vc.tensor_scalar(out=wtmp[:], in0=Rl[:], scalar1=Dl[:, mm:mm + 1],
                                 scalar2=0.0, op0=ALU.add, op1=ALU.min)
                vc.mark(vP_w)
                vc.wait_ge(sP_w, m + 1)
            vc.tensor_scalar(out=getmp[:], in0=Rl[:], scalar1=Dl[:, 0:1],
                             scalar2=0.0, op0=ALU.add, op1=ALU.is_le)
            vc.tensor_mul(out=Wt[:, 0:128], in0=Wt[:, 0:128], in1=getmp[:])
            for m in range(32):
                vc.tensor_scalar(out=Wg[:, 128 * m:128 * (m + 1)],
                                 in0=Wg[:, 128 * m:128 * (m + 1)],
                                 scalar1=scalc[:, 2:3], scalar2=None, op0=ALU.mult)

            for i in range(TS):
                xb = xbuf[:, (i % 2) * D:(i % 2 + 1) * D]
                vc.wait_ge(s_ld0 if i % 2 == 0 else s_ld1, 16 * (i // 2 + 1))
                st = stats[:, (i % 2) * 12:(i % 2) * 12 + 12]
                vc.bn_stats(out=st[:, 0:6], in_=xb[:, 0:512])
                vc.bn_stats(out=st[:, 6:12], in_=xb[:, 512:1024])
                vc.bn_aggr(out=mv[:, (i % 2) * 2:(i % 2) * 2 + 2], in_=st)
                vc.tensor_scalar(out=veps[:, i % 2:i % 2 + 1],
                                 in0=mv[:, (i % 2) * 2 + 1:(i % 2) * 2 + 2],
                                 scalar1=1e-5, scalar2=None, op0=ALU.add)
                vc.mark(vA_st)
                vc.wait_ge(sA_rs, 2 + i + 1)
                vc.tensor_scalar(out=V[:, D * i:D * (i + 1)], in0=xb,
                                 scalar1=mv[:, (i % 2) * 2:(i % 2) * 2 + 1],
                                 scalar2=rstd[:, i % 2:i % 2 + 1],
                                 op0=ALU.subtract, op1=ALU.mult)
                vc.mark(vA_nrm)
                for ct in range(NCT):
                    k = i * NCT + ct
                    vc.wait_ge(tA_tr, 1 + k + 1)
                    ps = psT0 if k % 2 == 0 else psT1
                    vc.tensor_scalar(out=K[:, FULL * ct + 128 * i: FULL * ct + 128 * (i + 1)],
                                     in0=ps[:, 0:128], scalar1=ks_s[:, ct:ct + 1],
                                     scalar2=None, op0=ALU.mult)
                    vc.mark(vA_ev)

            for q in range(TQ):
                for ct in range(NCT):
                    k = kkQ(q, ct)
                    vc.wait_ge(tA_tr, 1 + k + 1)
                    ps = psT0 if k % 2 == 0 else psT1
                    vc.tensor_scalar(out=Qt[:, 128 * ct:128 * (ct + 1)],
                                     in0=ps[:, 0:128], scalar1=qs_s[:, ct:ct + 1],
                                     scalar2=None, op0=ALU.mult)
                    vc.mark(vA_ev)
                for idx in range(first_pair_idx[q], last_pair_idx[q] + 1):
                    (qq, j, s, fi, la) = PAIRS[idx]
                    vc.wait_ge(tB_s, idx + 1)
                    if idx >= 2:
                        vc.wait_ge(tB_m2, idx - 1)
                    psS = psS0 if idx % 2 == 0 else psS1
                    m = s - q
                    wsl = Wt[:, 128 * m:128 * (m + 1)] if s < TQ else Wg[:, 128 * m:128 * (m + 1)]
                    vc.tensor_mul(out=sw[:, (idx % 2) * 128:(idx % 2 + 1) * 128],
                                  in0=psS[:, 0:128], in1=wsl)
                    vc.mark(vB_w)
                vc.wait_ge(tB_m2, last_pair_idx[q] + 1)
                vc.wait_ge(s_ld0 if q % 2 == 0 else s_ld1, 16 * (TS // 2 + q // 2 + 1))
                if q >= 2:
                    vc.wait_ge(s_out0 if q % 2 == 0 else s_out1, 16 * (q // 2))
                x0 = xbuf[:, (q % 2) * D:(q % 2 + 1) * D]
                xt = x1t[:, (q % 2) * D:(q % 2 + 1) * D]
                vc.scalar_tensor_tensor(out=xt[:, 0:512], in0=psA0[:],
                                        scalar=scalc[:, 1:2], in1=x0[:, 0:512],
                                        op0=ALU.mult, op1=ALU.add)
                vc.scalar_tensor_tensor(out=xt[:, 512:1024], in0=psA1[:],
                                        scalar=scalc[:, 1:2], in1=x0[:, 512:1024],
                                        op0=ALU.mult, op1=ALU.add)
                vc.mark(vB_x1)
                st = stats[:, (q % 2) * 12:(q % 2) * 12 + 12]
                vc.bn_stats(out=st[:, 0:6], in_=xt[:, 0:512])
                vc.bn_stats(out=st[:, 6:12], in_=xt[:, 512:1024])
                vc.bn_aggr(out=mv[:, (q % 2) * 2:(q % 2) * 2 + 2], in_=st)
                vc.tensor_scalar(out=veps[:, q % 2:q % 2 + 1],
                                 in0=mv[:, (q % 2) * 2 + 1:(q % 2) * 2 + 2],
                                 scalar1=1e-5, scalar2=None, op0=ALU.add)
                vc.mark(vA_st)
                vc.wait_ge(sA_rs, 2 + TS + q + 1)
                if q >= 1:
                    vc.wait_ge(tA_tr, 1 + kkL(q - 1, NCT - 1) + 1)
                vc.tensor_scalar(out=lnz[:], in0=xt,
                                 scalar1=mv[:, (q % 2) * 2:(q % 2) * 2 + 1],
                                 scalar2=rstd[:, q % 2:q % 2 + 1],
                                 op0=ALU.subtract, op1=ALU.mult)
                vc.mark(vB_ln)
                for ct in range(NCT):
                    k = kkL(q, ct)
                    vc.wait_ge(tA_tr, 1 + k + 1)
                    ps = psT0 if k % 2 == 0 else psT1
                    vc.tensor_copy(out=lnzT[:, 128 * ct:128 * (ct + 1)],
                                   in_=ps[:, 0:128])
                    vc.mark(vA_ev)
                vc.engine_nop()
                vc.mark(vB_lt)
                vc.wait_ge(tB_z, q + 1)
                vc.tensor_copy(out=z_sb[:, 8 * q:8 * (q + 1)], in_=psZ[:, 0:8])
                vc.mark(vB_z)

    return nc


def prep_launch1_inputs(x, q_scale, k_scale, decay_logit, out_scale, proj_w):
    qs_cm = np.ascontiguousarray(np.asarray(q_scale, np.float32).reshape(NCT, 128).T)
    ks_cm = np.ascontiguousarray(np.asarray(k_scale, np.float32).reshape(NCT, 128).T)
    pT = np.asarray(proj_w, np.float32).T
    projT_h = np.ascontiguousarray(pT.reshape(NCT, 128, 8).transpose(1, 0, 2).reshape(128, NCT * 8))
    maps = []
    for c in range(8):
        b, p = c // 2, c % 2
        A, B = x[b, 0:2048], x[b, 2048:4096]
        xf = np.concatenate([A, B], axis=0) if p == 0 else np.concatenate([B, B], axis=0)
        scal = np.array([[float(decay_logit), float(out_scale), 1.0 - p, 0.0]], np.float32)
        maps.append({"x_full": np.ascontiguousarray(xf, np.float32),
                     "qs_cm": qs_cm, "ks_cm": ks_cm, "scal": scal, "projT": projT_h})
    return maps


"""Launch 2/3 builder: global router + capacity bisection + dense MoE (+ attn_b).

Token frame (NT): [halo 128 (L2 only) | own 2048], time-ordered.
MLP units: 4 shared quarters + 16 expert halves, each [1024 -> 1024(mid) -> 1024].
See NOTES.md. All host-prep in prep_ffn_inputs().
"""
import numpy as np
import ml_dtypes
import concourse.bass as bass
import concourse.mybir as mybir

F32 = mybir.dt.float32
BF16 = mybir.dt.bfloat16
U32 = mybir.dt.uint32
AF = mybir.ActivationFunctionType
ALU = mybir.AluOpType
AX = mybir.AxisListType

D = 1024
NCT = 8
OWN = 2048
NALL = 16384
NZT = NALL // 128
CAP = 5120.0
NITER = 60
bfd = ml_dtypes.bfloat16

UNITS = [("sh", q, None) for q in range(4)] + [("ex", e, h) for e in range(8) for h in range(2)]
NU = len(UNITS)


class Sem:
    def __init__(self, handle):
        self.h = handle
        self.n = 0

    def bump(self, by=1):
        self.n += by
        return self.n


class Rot:

    def __init__(self, tensors):
        self.t = tensors
        self.free = {}
        self.g = 0

    def acquire(self):
        g = self.g
        self.g += 1
        return self.t[g % len(self.t)], self.free.get(g - len(self.t)), g

    def release(self, g, sem, n):
        self.free[g] = (sem, n)


def build_ffn_launch(with_attn_b):
    NH = 128 if with_attn_b else 0
    NT = NH + OWN
    NTT = NT // 128
    CHUNKS = ([(0, 128)] if NH else []) + [(NH + 256 * k, 256) for k in range(8)]
    NCH = len(CHUNKS)

    nc = bass.Bass("TRN2", target_bir_lowering=False)

    def din(name, shape):
        return nc.dram_tensor(name, shape, F32, kind="ExternalInput").ap()

    xrh = din("xrh", [NT, D])
    zall = din("zall", [128, NZT * 8])
    pb8 = din("pb8", [128, 8])
    eb8 = din("eb8", [128, 8])
    u1w = din("u1w", [NU, D, D])
    u2w = din("u2w", [NU, D, D])
    ubias = din("ubias", [128, NU * 8])
    fc2bT = din("fc2bT", [8, D])
    sh2b_row = din("sh2b_row", [1, D])
    if with_attn_b:
        in_w = din("in_w", [D, D])
        in_b_cm = din("in_b_cm", [128, 8])
        mod_w1 = din("mod_w1", [D, 4096])
        mod_b1_cm = din("mod_b1_cm", [128, 32])
        mod_w2 = din("mod_w2", [4096, D])
        mod_b2_cm = din("mod_b2_cm", [128, 8])
        kern_cm = din("kern_cm", [128, 8 * 64])
        out_w = din("out_w", [D, D])
        out_b_cm = din("out_b_cm", [128, 8])
        projT = din("projT", [128, NCT * 8])
        x3_out = nc.dram_tensor("x3_out", [OWN, D], F32, kind="ExternalOutput").ap()
        z3_out = nc.dram_tensor("z3_out", [OWN, 8], F32, kind="ExternalOutput").ap()
        x2_dram = nc.dram_tensor("x2_dram", [NT, D], F32).ap()
        y_out = None
    else:
        y_out = nc.dram_tensor("y_out", [OWN, D], F32, kind="ExternalOutput").ap()
    wdram = nc.dram_tensor("wdram", [128, 256], F32).ap()
    edram = nc.dram_tensor("edram", [128, 256], F32).ap()

    identb = nc.inline_tensor(np.eye(128, dtype=bfd), "identb").ap()
    ecol_np = (np.arange(128)[:, None] // 16).astype(np.float32)
    Esel_np = (ecol_np == np.arange(8)[None, :]).astype(np.float32)
    e_colC = nc.inline_tensor(ecol_np, "e_colC").ap()
    EselC = nc.inline_tensor(Esel_np, "EselC").ap()
    EselTC = nc.inline_tensor(np.ascontiguousarray(Esel_np.T), "EselTC").ap()
    onesC = nc.inline_tensor(np.ones((1, 128), np.float32), "onesC").ap()
    ones512C = nc.inline_tensor(np.ones((1, 512), bfd), "ones512C").ap()
    iota8C = nc.inline_tensor(np.broadcast_to(np.arange(8).astype(np.float32)[None, :],
                                              (128, 8)).copy(), "iota8C").ap()
    ident8C = nc.inline_tensor(np.eye(8, dtype=np.float32), "ident8C").ap()
    ones8x128C = nc.inline_tensor(np.ones((8, 128), np.float32), "ones8x128C").ap()
    selC = []
    for e in range(8):
        m = np.zeros((8, 128), bfd)
        m[e, :] = 1.0
        selC.append(nc.inline_tensor(m, f"sel{e}C").ap())

    from contextlib import ExitStack
    with ExitStack() as ctx:
        E = ctx.enter_context
        cmA = E(nc.sbuf_tensor("cmA", [128, NCT * NT], BF16))
        y_cm = E(nc.sbuf_tensor("y_cm", [128, NCT * NT], BF16))
        fc1sb = E(nc.sbuf_tensor("fc1sb", [128, 2 * NCT * 1024], BF16))
        fc2sb = E(nc.sbuf_tensor("fc2sb", [128, 2 * NCT * 1024], BF16))
        sall = E(nc.sbuf_tensor("sall", [128, NZT * 8], F32))
        wgrid = E(nc.sbuf_tensor("wgrid", [128, 256], F32))
        egrid = E(nc.sbuf_tensor("egrid", [128, 256], F32))
        hbuf = E(nc.sbuf_tensor("hbuf", [128, NCT * 256], BF16))
        hraw = E(nc.sbuf_tensor("hraw", [128, 2 * 256], BF16))
        gate_sb = E(nc.sbuf_tensor("gate_sb", [128, 2 * 256], BF16))
        xbuf = E(nc.sbuf_tensor("xbuf", [128, 2 * D], F32))
        x2t = E(nc.sbuf_tensor("x2t", [128, 2 * D], F32))
        wge = x2t
        epe = xbuf
        ubias_s = E(nc.sbuf_tensor("ubias_s", [128, NU * 8], F32))
        fc2bT_s = E(nc.sbuf_tensor("fc2bT_s", [8, D], BF16))
        sh2b_s = E(nc.sbuf_tensor("sh2b_s", [1, D], BF16))
        pb_s = E(nc.sbuf_tensor("pb_s", [128, 8], F32))
        eb_s = E(nc.sbuf_tensor("eb_s", [128, 8], F32))
        identt = E(nc.sbuf_tensor("identt", [128, 128], BF16))
        ecol_s = E(nc.sbuf_tensor("ecol_s", [128, 1], F32))
        esel_s = E(nc.sbuf_tensor("esel_s", [128, 8], F32))
        eselT_s = E(nc.sbuf_tensor("eselT_s", [8, 128], F32))
        ones_s = E(nc.sbuf_tensor("ones_s", [1, 128], F32))
        ones512_s = E(nc.sbuf_tensor("ones512_s", [1, 512], BF16))
        iota8_s = E(nc.sbuf_tensor("iota8_s", [128, 8], F32))
        ident8_s = E(nc.sbuf_tensor("ident8_s", [8, 8], F32))
        ones8x128_s = E(nc.sbuf_tensor("ones8x128_s", [8, 128], F32))
        sel_s = E(nc.sbuf_tensor("sel_s", [8, 8 * 128], BF16))
        m8 = E(nc.sbuf_tensor("m8", [128, 8], F32))
        i8 = E(nc.sbuf_tensor("i8", [128, 8], U32))
        rtmp = E(nc.sbuf_tensor("rtmp", [128, 16], F32))
        lohi = E(nc.sbuf_tensor("lohi", [8, 8], F32))
        cnt_col = E(nc.sbuf_tensor("cnt_col", [128, 1], F32))
        lo_bc = E(nc.sbuf_tensor("lo_bc", [128, 8], F32))
        X8 = E(nc.sbuf_tensor("X8", [8, 8], F32))
        g8all = E(nc.sbuf_tensor("g8all", [8, NT], BF16))
        gtm = E(nc.sbuf_tensor("gtm", [128, 8], BF16))
        geq = E(nc.sbuf_tensor("geq", [128, 2 * 8], F32))
        stats = E(nc.sbuf_tensor("stats", [128, 2 * 12], F32))
        mv = E(nc.sbuf_tensor("mv", [128, 2 * 2], F32))
        veps = E(nc.sbuf_tensor("veps", [128, 2], F32))
        rstd = E(nc.sbuf_tensor("rstd", [128, 2], F32))
        lnt = E(nc.sbuf_tensor("lnt", [128, D], BF16))
        if not with_attn_b:
            zsb_t = E(nc.sbuf_tensor("zsb", [128, NZT * 8], F32))
            zsb = zsb_t[:]
        else:
            lnchunk = E(nc.sbuf_tensor("lnchunk", [128, NCT * 512], BF16))
            zsb = lnchunk[:].bitcast(F32)[:, 0:NZT * 8]
            kern_s = E(nc.sbuf_tensor("kern_s", [128, 8 * 64], F32))
            inb_s = E(nc.sbuf_tensor("inb_s", [128, 8], F32))
            modb1_s = E(nc.sbuf_tensor("modb1_s", [128, 32], F32))
            modb2_s = E(nc.sbuf_tensor("modb2_s", [128, 8], F32))
            outb_s = E(nc.sbuf_tensor("outb_s", [128, 8], F32))
            projT_s = E(nc.sbuf_tensor("projT_s", [128, NCT * 8], BF16))
            z_sb = E(nc.sbuf_tensor("z_sb", [128, 16 * 8], F32))
            wstream = E(nc.sbuf_tensor("wstream", [128, 2 * 32 * 128], BF16))
            mscr = sall[:, 0:512]
            lnzT = fc1sb[:, 9216:10240]
            modu = cmA
            wfix_in = y_cm[:, 0:8192]
            wfix_out = y_cm[:, 8192:16384]
            xproj = fc2sb[:, 0:NCT * 512]
            conv_full = fc2sb
            diag64 = fc1sb[:, 0:8192]
            osb = fc1sb[:, 12288:16384]
            x3t = x2t

        ps0 = E(nc.psum_tensor("ps0", [128, 512], F32))
        ps1 = E(nc.psum_tensor("ps1", [128, 512], F32))
        psY = [E(nc.psum_tensor(f"psY{i}", [128, 512], F32)) for i in range(4)]
        psT0 = E(nc.psum_tensor("psT0", [128, 1024], BF16))
        psT1 = E(nc.psum_tensor("psT1", [128, 1024], BF16))

        def psYv(ct, size):
            return psY[ct // 2][:, (ct % 2) * 256:(ct % 2) * 256 + size]

        def S(name):
            return Sem(E(nc.semaphore(name)))

        d_pre = S("d_pre")
        d_z = S("d_z")
        d_x = [S("d_x0"), S("d_x1")]
        d_w1 = [S("d_w1a"), S("d_w1b")]
        d_w2 = [S("d_w2a"), S("d_w2b")]
        d_gr = S("d_gr")
        d_out = [S("d_out0"), S("d_out1")]
        d_zo = S("d_zo")
        d_x2w = [S("d_x2w0"), S("d_x2w1")]
        d_wm = [S("d_wm0"), S("d_wm1")]
        d_wf = S("d_wf")
        t_tr = S("t_tr")
        t_mm = S("t_mm")
        t_y = S("t_y")
        t_cnt = S("t_cnt")
        t_thr = S("t_thr")
        t_x8 = S("t_x8")
        t_cv = S("t_cv")
        t_z = S("t_z")
        v_ln = S("v_ln")
        v_ev = S("v_ev")
        v_st = S("v_st")
        s_rs = S("s_rs")
        v_zb = S("v_zb")
        s_sg = S("s_sg")
        v_gd = S("v_gd")
        v_it = S("v_it")
        v_gt = S("v_gt")
        v_g8 = S("v_g8")
        v_gc = S("v_gc")
        v_h = S("v_h")
        s_h = S("s_h")
        v_y = S("v_y")
        v_x2 = S("v_x2")
        v_diag = S("v_diag")
        v_cs = S("v_cs")
        v_mo = S("v_mo")
        v_ps = S("v_ps")
        v_x3 = S("v_x3")
        v_z = S("v_z")
        chV = E(nc.semaphore("chV"))
        chS = E(nc.semaphore("chS"))

        OPS = {e: [] for e in ("sync", "gpsimd", "tensor", "vector", "scalar")}

        def op(eng, fn):
            OPS[eng].append(fn)

        wslot_guard = {}
        rotA = Rot([ps0, ps1])
        rotT = Rot([psT0, psT1])
        rotC = Rot([psY[0], psY[1]])

        def wait_guard(e, guard):
            if guard is not None:
                e.wait_ge(guard[0].h if isinstance(guard[0], Sem) else guard[0], guard[1])

        pre_list = [
            (identt[:], identb), (ecol_s[:], e_colC), (esel_s[:], EselC),
            (eselT_s[:], EselTC), (ones_s[:], onesC), (ones512_s[:], ones512C),
            (iota8_s[:], iota8C), (ident8_s[:], ident8C), (ones8x128_s[:], ones8x128C),
            (ubias_s[:], ubias), (fc2bT_s[:], fc2bT), (sh2b_s[:], sh2b_row),
            (pb_s[:], pb8), (eb_s[:], eb8),
        ] + [(sel_s[:, 128 * e:128 * (e + 1)], selC[e]) for e in range(8)]
        if with_attn_b:
            pre_list += [(kern_s[:], kern_cm), (inb_s[:], in_b_cm),
                         (modb1_s[:], mod_b1_cm), (modb2_s[:], mod_b2_cm),
                         (outb_s[:], out_b_cm), (projT_s[:], projT)]

        for dst, src in pre_list:
            d_pre.bump(16)

            def f(g, dst=dst, src=src):
                g.dma_start(out=dst, in_=src).then_inc(d_pre.h, 16)
            op("gpsimd", f)
        NPRE = d_pre.n
        d_z.bump(16)

        def fz(g):
            g.dma_start(out=zsb, in_=zall).then_inc(d_z.h, 16)
        op("gpsimd", fz)

        def emit_ln_pass(src_dram, cm_dst, sync_pre_waits):
            for ti in range(NTT):
                par = ti % 2
                dsem = d_x[par]
                nld = dsem.bump(16)
                nln_prev = v_ln.n

                def fs(sy, ti=ti, par=par, dsem=dsem, nld=nld, nln_prev=nln_prev,
                       src_dram=src_dram, spw=(sync_pre_waits if ti == 0 else [])):
                    for sem, val in spw:
                        sy.wait_ge(sem.h, val)
                    if nln_prev >= 2:
                        sy.wait_ge(v_ln.h, nln_prev - 1)
                    sy.dma_start(out=xbuf[:, par * D:(par + 1) * D],
                                 in_=src_dram[128 * ti:128 * (ti + 1), :]).then_inc(dsem.h, 16)
                op("sync", fs)

                nst = v_st.bump()

                def fv1(v, par=par, nld=nld, dsem=dsem):
                    v.wait_ge(dsem.h, nld)
                    xb = xbuf[:, par * D:(par + 1) * D]
                    st = stats[:, par * 12:par * 12 + 12]
                    v.bn_stats(out=st[:, 0:6], in_=xb[:, 0:512])
                    v.bn_stats(out=st[:, 6:12], in_=xb[:, 512:1024])
                    v.bn_aggr(out=mv[:, par * 2:par * 2 + 2], in_=st)
                    v.tensor_scalar(out=veps[:, par:par + 1],
                                    in0=mv[:, par * 2 + 1:par * 2 + 2],
                                    scalar1=1e-5, scalar2=None, op0=ALU.add)
                    v.mark(v_st.h)
                op("vector", fv1)

                nrs = s_rs.bump()

                def fsc(s, par=par, nst=nst, nln_prev=nln_prev):
                    s.wait_ge(v_st.h, nst)
                    if nln_prev >= 1:
                        s.wait_ge(v_ln.h, nln_prev)
                    s.activation(out=rstd[:, par:par + 1], in_=veps[:, par:par + 1], func=AF.Ln)
                    s.activation(out=rstd[:, par:par + 1], in_=rstd[:, par:par + 1],
                                 func=AF.Exp, scale=-0.5)
                    s.mark(s_rs.h)
                op("scalar", fsc)

                tr0 = t_tr.n
                nln = v_ln.bump()

                def fv2(v, par=par, nrs=nrs, tr0=tr0):
                    v.wait_ge(s_rs.h, nrs)
                    if tr0 > 0:
                        v.wait_ge(t_tr.h, tr0)
                    xb = xbuf[:, par * D:(par + 1) * D]
                    v.tensor_scalar(out=lnt[:], in0=xb,
                                    scalar1=mv[:, par * 2:par * 2 + 1],
                                    scalar2=rstd[:, par:par + 1],
                                    op0=ALU.subtract, op1=ALU.mult)
                    v.mark(v_ln.h)
                op("vector", fv2)

                for ct in range(NCT):
                    ps, guard, g = rotT.acquire()
                    ntr = t_tr.bump()

                    def ft(t, ct=ct, ps=ps, guard=guard, nln=nln):
                        if ct == 0:
                            t.wait_ge(v_ln.h, nln)
                        wait_guard(t, guard)
                        t.transpose(ps[:, 0:128], lnt[:, 128 * ct:128 * (ct + 1)],
                                    identt[:]).then_inc(t_tr.h, 1)
                    op("tensor", ft)

                    nev = v_ev.bump()

                    def fv3(v, ti=ti, ct=ct, ps=ps, ntr=ntr, cm_dst=cm_dst):
                        v.wait_ge(t_tr.h, ntr)
                        v.tensor_copy(out=cm_dst[:, NT * ct + 128 * ti: NT * ct + 128 * (ti + 1)],
                                      in_=ps[:, 0:128])
                        v.mark(v_ev.h)
                    op("vector", fv3)
                    rotT.release(g, v_ev, nev)

        emit_ln_pass(xrh, cmA, [(d_pre, NPRE)])
        PH1_EV = v_ev.n

        nzb = v_zb.bump()

        def fvr(v):
            v.wait_ge(d_z.h, 16)
            v.wait_ge(d_pre.h, NPRE)
            for zt in range(NZT):
                v.tensor_add(out=sall[:, 8 * zt:8 * (zt + 1)],
                             in0=zsb[:, 8 * zt:8 * (zt + 1)], in1=pb_s[:])
            v.mark(v_zb.h)
        op("vector", fvr)
        nsg = s_sg.bump()

        def fsr(s, nzb=nzb):
            s.wait_ge(v_zb.h, nzb)
            s.activation(out=zsb, in_=sall[:], func=AF.Sigmoid)
            s.mark(s_sg.h)
        op("scalar", fsr)

        def fvr2(v, nsg=nsg):
            v.wait_ge(s_sg.h, nsg)
            for zt in range(NZT):
                v.tensor_add(out=sall[:, 8 * zt:8 * (zt + 1)],
                             in0=zsb[:, 8 * zt:8 * (zt + 1)], in1=eb_s[:])
            v.tensor_scalar(out=sall[:], in0=sall[:], scalar1=1e-12, scalar2=None, op0=ALU.max)
        op("vector", fvr2)

        for zt in range(NZT):
            def fvr3(v, zt=zt):
                sl = sall[:, 8 * zt:8 * (zt + 1)]
                v.max(out=m8[:], in_=sl)
                v.max_index(out=i8[:], in_max=m8[:], in_values=sl)
                v.tensor_copy(out=egrid[:, 2 * zt:2 * zt + 2], in_=i8[:, 0:2])
                v.tensor_add(out=rtmp[:, 0:1], in0=m8[:, 0:1], in1=m8[:, 1:2])
                v.tensor_scalar(out=rtmp[:, 1:2], in0=rtmp[:, 0:1], scalar1=1e-12,
                                scalar2=None, op0=ALU.max)
                v.reciprocal(out=rtmp[:, 2:3], in_=rtmp[:, 1:2])
                v.tensor_mul(out=wgrid[:, 2 * zt:2 * zt + 1], in0=m8[:, 0:1], in1=rtmp[:, 2:3])
                v.tensor_mul(out=wgrid[:, 2 * zt + 1:2 * zt + 2], in0=m8[:, 1:2], in1=rtmp[:, 2:3])
            op("vector", fvr3)
        ngd = v_gd.bump()

        def fvr4(v):
            v.mark(v_gd.h)
        op("vector", fvr4)
        NGD = v_gd.n

        n_gr1 = d_gr.bump(32)

        def fgb(g):
            g.wait_ge(v_gd.h, NGD)
            g.dma_start(out=wdram, in_=wgrid[:]).then_inc(d_gr.h, 16)
            g.dma_start(out=edram, in_=egrid[:]).then_inc(d_gr.h, 16)
        op("gpsimd", fgb)
        n_gr2 = d_gr.bump(16 * 16)

        def fgb2(g, n_gr1=n_gr1):
            g.wait_ge(d_gr.h, n_gr1)
            for e in range(8):
                g.dma_start(out=wge[16 * e:16 * (e + 1), 0:2048],
                            in_=wdram.rearrange("(a b) f -> a (b f)", a=16)).then_inc(d_gr.h, 16)
                g.dma_start(out=epe[16 * e:16 * (e + 1), 0:2048],
                            in_=edram.rearrange("(a b) f -> a (b f)", a=16)).then_inc(d_gr.h, 16)
        op("gpsimd", fgb2)

        nit0 = v_it.bump()

        def fvb(v, n_gr2=n_gr2):
            v.wait_ge(d_gr.h, n_gr2)
            v.tensor_scalar(out=epe[:, 0:2048], in0=epe[:, 0:2048], scalar1=ecol_s[:],
                            scalar2=None, op0=ALU.is_equal)
            v.tensor_scalar(out=wge[:, 0:2048], in0=wge[:, 0:2048], scalar1=1.0,
                            scalar2=None, op0=ALU.add)
            v.tensor_mul(out=wge[:, 0:2048], in0=wge[:, 0:2048], in1=epe[:, 0:2048])
            v.tensor_scalar(out=wge[:, 0:2048], in0=wge[:, 0:2048], scalar1=1.0,
                            scalar2=None, op0=ALU.subtract)
            v.memset(lohi[:, 0:8], 0.0)
            v.memset(lohi[:, 2:3], 1.0)
            v.memset(lohi[:, 4:5], 0.5)
            v.mark(v_it.h)
        op("vector", fvb)

        for k in range(NITER):
            nthr = t_thr.bump()

            def ftb1(t, k=k, req=v_it.n):
                t.wait_ge(v_it.h, req)
                t.matmul(ps1[:, 0:1], eselT_s[:], lohi[:, 4:5],
                         start=True, stop=True).then_inc(t_thr.h, 1)
            op("tensor", ftb1)

            nv1 = v_it.bump()

            def fvb1(v, nthr=nthr):
                v.wait_ge(t_thr.h, nthr)
                v.tensor_scalar(out=epe[:, 0:2048], in0=wge[:, 0:2048], scalar1=ps1[:, 0:1],
                                scalar2=None, op0=ALU.is_gt, op1=ALU.add,
                                accum_out=cnt_col[:])
                v.mark(v_it.h)
            op("vector", fvb1)

            ncnt = t_cnt.bump()

            def ftb2(t, nv1=nv1):
                t.wait_ge(v_it.h, nv1)
                t.matmul(ps0[0:8, 0:1], esel_s[:], cnt_col[:],
                         start=True, stop=True).then_inc(t_cnt.h, 1)
            op("tensor", ftb2)

            nv2 = v_it.bump()

            def fvb2(v, k=k, ncnt=ncnt):
                v.wait_ge(t_cnt.h, ncnt)
                lo = lohi[:, k % 2:k % 2 + 1]
                lon = lohi[:, (k + 1) % 2:(k + 1) % 2 + 1]
                hi = lohi[:, 2 + k % 2:3 + k % 2]
                hin = lohi[:, 2 + (k + 1) % 2:3 + (k + 1) % 2]
                ge = lohi[:, 5:6]
                gn = lohi[:, 6:7]
                tmp = lohi[:, 7:8]
                mid = lohi[:, 4:5]
                v.tensor_scalar(out=ge, in0=ps0[0:8, 0:1], scalar1=CAP,
                                scalar2=None, op0=ALU.is_ge)
                v.tensor_scalar(out=gn, in0=ge, scalar1=-1.0, scalar2=1.0,
                                op0=ALU.mult, op1=ALU.add)
                v.tensor_mul(out=tmp, in0=ge, in1=mid)
                v.tensor_mul(out=lon, in0=gn, in1=lo)
                v.tensor_add(out=lon, in0=lon, in1=tmp)
                v.tensor_mul(out=tmp, in0=ge, in1=hi)
                v.tensor_mul(out=hin, in0=gn, in1=mid)
                v.tensor_add(out=hin, in0=hin, in1=tmp)
                v.tensor_add(out=tmp, in0=lon, in1=hin)
                v.tensor_scalar(out=mid, in0=tmp, scalar1=0.5, scalar2=None, op0=ALU.mult)
                v.mark(v_it.h)
            op("vector", fvb2)

        fin = NITER % 2
        nx8v = v_it.bump()

        def fvx(v, fin=fin):
            v.tensor_scalar(out=X8[:], in0=ident8_s[:], scalar1=lohi[:, fin:fin + 1],
                            scalar2=None, op0=ALU.mult)
            v.mark(v_it.h)
        op("vector", fvx)
        nx8 = t_x8.bump()

        def ftx(t, nx8v=nx8v):
            t.wait_ge(v_it.h, nx8v)
            t.matmul(psY[3][:, 0:8], ones8x128_s[:], X8[:],
                     start=True, stop=True).then_inc(t_x8.h, 1)
        op("tensor", ftx)

        def fvx2(v, nx8=nx8):
            v.wait_ge(t_x8.h, nx8)
            v.tensor_copy(out=lo_bc[:], in_=psY[3][:, 0:8])
        op("vector", fvx2)

        for ti in range(NTT):
            ngt = v_gt.bump()

            def fvg(v, ti=ti):
                v.tensor_scalar(out=geq[:, 0:8], in0=iota8_s[:],
                                scalar1=egrid[:, 2 * ti:2 * ti + 1],
                                scalar2=None, op0=ALU.is_equal)
                v.tensor_scalar(out=geq[:, 8:16], in0=iota8_s[:],
                                scalar1=egrid[:, 2 * ti + 1:2 * ti + 2],
                                scalar2=None, op0=ALU.is_equal)
                v.tensor_mul(out=rtmp[:, 8:16], in0=geq[:, 0:8], in1=lo_bc[:])
                v.reduce_sum(out=rtmp[:, 0:1], in_=rtmp[:, 8:16], axis=AX.X)
                v.tensor_mul(out=rtmp[:, 8:16], in0=geq[:, 8:16], in1=lo_bc[:])
                v.reduce_sum(out=rtmp[:, 1:2], in_=rtmp[:, 8:16], axis=AX.X)
                v.tensor_tensor(out=rtmp[:, 2:3], in0=wgrid[:, 2 * ti:2 * ti + 1],
                                in1=rtmp[:, 0:1], op=ALU.is_gt)
                v.tensor_tensor(out=rtmp[:, 3:4], in0=wgrid[:, 2 * ti + 1:2 * ti + 2],
                                in1=rtmp[:, 1:2], op=ALU.is_gt)
                v.tensor_mul(out=rtmp[:, 4:5], in0=wgrid[:, 2 * ti:2 * ti + 1],
                             in1=rtmp[:, 2:3])
                v.tensor_mul(out=rtmp[:, 5:6], in0=wgrid[:, 2 * ti + 1:2 * ti + 2],
                             in1=rtmp[:, 3:4])
                v.tensor_scalar(out=geq[:, 0:8], in0=geq[:, 0:8], scalar1=rtmp[:, 4:5],
                                scalar2=None, op0=ALU.mult)
                v.tensor_scalar(out=geq[:, 8:16], in0=geq[:, 8:16], scalar1=rtmp[:, 5:6],
                                scalar2=None, op0=ALU.mult)
                v.tensor_add(out=gtm[:], in0=geq[:, 0:8], in1=geq[:, 8:16])
                v.mark(v_gt.h)
            op("vector", fvg)

            ps, guard, g = rotT.acquire()
            ntr = t_tr.bump()

            def ftg(t, ps=ps, guard=guard, ngt=ngt):
                t.wait_ge(v_gt.h, ngt)
                wait_guard(t, guard)
                t.transpose(ps[0:8, 0:128], gtm[:], identt[:]).then_inc(t_tr.h, 1)
            op("tensor", ftg)

            ng8 = v_g8.bump()

            def fvg8(v, ti=ti, ps=ps, ntr=ntr):
                v.wait_ge(t_tr.h, ntr)
                v.tensor_copy(out=g8all[:, 128 * ti:128 * (ti + 1)], in_=ps[0:8, 0:128])
                v.mark(v_g8.h)
            op("vector", fvg8)
            rotT.release(g, v_g8, ng8)
        NG8 = v_g8.n

        def emit_units():
            for ui, unit in enumerate(UNITS):
                kind = unit[0]
                b = ui % 2
                prev_ty = max(0, (ui - 1) * NCH)
                nw1 = d_w1[b].bump(16)
                nw2 = d_w2[b].bump(16)

                def fgw(g, ui=ui, b=b, prev_ty=prev_ty):
                    if ui >= 2:
                        g.wait_ge(t_y.h, prev_ty)
                    g.dma_start(out=fc1sb[:, b * 8192:(b + 1) * 8192].rearrange(
                                    "p (ct m) -> p ct m", m=1024),
                                in_=u1w[ui].rearrange("(ct p) m -> p ct m", p=128)
                                ).then_inc(d_w1[b].h, 16)
                    g.dma_start(out=fc2sb[:, b * 8192:(b + 1) * 8192].rearrange(
                                    "p (mt c) -> p mt c", c=1024),
                                in_=u2w[ui].rearrange("(mt p) c -> p mt c", p=128)
                                ).then_inc(d_w2[b].h, 16)
                op("gpsimd", fgw)

                for ci, (cst, csz) in enumerate(CHUNKS):
                    gidx = ui * NCH + ci
                    if kind == "ex":
                        e = unit[1]
                        psg, guard, gg = rotA.acquire()
                        ngmm = t_mm.bump()

                        def ftgate(t, e=e, psg=psg, guard=guard, cst=cst, csz=csz,
                                   first=(gidx == 0)):
                            if first:
                                t.wait_ge(v_g8.h, NG8)
                            wait_guard(t, guard)
                            t.matmul(psg[:, 0:csz], sel_s[:, 128 * e:128 * (e + 1)],
                                     g8all[:, cst:cst + csz],
                                     start=True, stop=True).then_inc(t_mm.h, 1)
                        op("tensor", ftgate)

                        ngc = v_gc.bump()

                        def fvgate(v, psg=psg, ngmm=ngmm, csz=csz, ci=ci):
                            v.wait_ge(t_mm.h, ngmm)
                            v.tensor_copy(out=gate_sb[:, (ci % 2) * 256:(ci % 2) * 256 + csz],
                                          in_=psg[:, 0:csz])
                            v.mark(v_gc.h)
                        op("vector", fvgate)
                        rotA.release(gg, v_gc, ngc)

                    for mt in range(NCT):
                        psm, guard, gg = rotA.acquire()
                        nmm = t_mm.bump()

                        def ftf1(t, ui=ui, b=b, mt=mt, psm=psm, guard=guard,
                                 cst=cst, csz=csz, nw1=nw1, nw2=nw2, gidx=gidx, ci=ci):
                            if ci == 0 and mt == 0:
                                t.wait_ge(d_w1[b].h, nw1)
                                t.wait_ge(d_w2[b].h, nw2)
                                if gidx == 0:
                                    t.wait_ge(v_ev.h, PH1_EV)
                            wait_guard(t, guard)
                            for ct in range(NCT):
                                mi = t.matmul(psm[:, 0:csz],
                                              fc1sb[:, b * 8192 + ct * 1024 + 128 * mt:
                                                    b * 8192 + ct * 1024 + 128 * (mt + 1)],
                                              cmA[:, NT * ct + cst: NT * ct + cst + csz],
                                              start=(ct == 0), stop=(ct == NCT - 1))
                            mi.then_inc(t_mm.h, 1)
                        op("tensor", ftf1)

                        nsh = s_h.bump()
                        if kind == "sh":
                            hdst = hbuf[:, mt * 256:mt * 256 + csz]
                        else:
                            hdst = hraw[:, (mt % 2) * 256:(mt % 2) * 256 + csz]

                        def fsg(s, psm=psm, nmm=nmm, hdst=hdst, ui=ui, mt=mt, csz=csz,
                                gidx=gidx, kind=kind, prev_vh=v_h.n, prev_ty=max(0, gidx)):
                            s.wait_ge(t_mm.h, nmm)
                            if kind == "sh" and gidx >= 1 and mt == 0:
                                s.wait_ge(t_y.h, prev_ty)
                            if kind == "ex" and prev_vh >= 2:
                                s.wait_ge(v_h.h, prev_vh - 1)
                            s.activation(out=hdst, in_=psm[:, 0:csz], func=AF.Gelu,
                                         bias=ubias_s[:, ui * 8 + mt:ui * 8 + mt + 1],
                                         scale=1.0)
                            s.mark(s_h.h)
                        op("scalar", fsg)
                        rotA.release(gg, s_h, nsh)

                        if kind == "ex":
                            nvh = v_h.bump()

                            def fvh(v, mt=mt, ci=ci, csz=csz, nsh=nsh, ngc=v_gc.n,
                                    gidx=gidx, prev_ty=max(0, gidx)):
                                v.wait_ge(s_h.h, nsh)
                                v.wait_ge(v_gc.h, ngc)
                                if gidx >= 1 and mt == 0:
                                    v.wait_ge(t_y.h, prev_ty)
                                v.tensor_mul(out=hbuf[:, mt * 256:mt * 256 + csz],
                                             in0=hraw[:, (mt % 2) * 256:(mt % 2) * 256 + csz],
                                             in1=gate_sb[:, (ci % 2) * 256:(ci % 2) * 256 + csz])
                                v.mark(v_h.h)
                            op("vector", fvh)

                    h_sem, h_n = (s_h, s_h.n) if kind == "sh" else (v_h, v_h.n)
                    nty = t_y.bump()

                    def ftf2(t, ui=ui, b=b, cst=cst, csz=csz, gidx=gidx,
                             h_sem=h_sem, h_n=h_n):
                        t.wait_ge(h_sem.h, h_n)
                        if gidx >= 1:
                            t.wait_ge(v_y.h, gidx)
                        for ct in range(NCT):
                            for mt in range(NCT):
                                mi = t.matmul(psYv(ct, csz),
                                              fc2sb[:, b * 8192 + mt * 1024 + 128 * ct:
                                                    b * 8192 + mt * 1024 + 128 * (ct + 1)],
                                              hbuf[:, mt * 256:mt * 256 + csz],
                                              start=(mt == 0), stop=(mt == NCT - 1))
                        mi.then_inc(t_y.h, 1)
                    op("tensor", ftf2)

                    nvy = v_y.bump()

                    def fvy(v, ui=ui, cst=cst, csz=csz, nty=nty):
                        v.wait_ge(t_y.h, nty)
                        for ct in range(NCT):
                            dst = y_cm[:, NT * ct + cst: NT * ct + cst + csz]
                            if ui == 0:
                                v.tensor_copy(out=dst, in_=psYv(ct, csz))
                            else:
                                v.tensor_add(out=dst, in0=dst, in1=psYv(ct, csz))
                        v.mark(v_y.h)
                    op("vector", fvy)

        emit_units()
        TMM_UNITS = t_mm.n

        def emit_final():
            for ci, (cst, csz) in enumerate(CHUNKS):
                gidx = NU * NCH + ci
                nty = t_y.bump()

                def ftf(t, cst=cst, csz=csz, gidx=gidx):
                    t.wait_ge(v_y.h, gidx)
                    for ct in range(NCT):
                        t.matmul(psYv(ct, csz), fc2bT_s[0:8, 128 * ct:128 * (ct + 1)],
                                 g8all[:, cst:cst + csz], start=True, stop=False)
                        mi = t.matmul(psYv(ct, csz), sh2b_s[0:1, 128 * ct:128 * (ct + 1)],
                                      ones512_s[0:1, 0:csz], start=False, stop=True)
                    mi.then_inc(t_y.h, 1)
                op("tensor", ftf)

                nvy = v_y.bump()

                def fvy(v, cst=cst, csz=csz, nty=nty):
                    v.wait_ge(t_y.h, nty)
                    for ct in range(NCT):
                        dst = y_cm[:, NT * ct + cst: NT * ct + cst + csz]
                        v.tensor_add(out=dst, in0=dst, in1=psYv(ct, csz))
                    v.mark(v_y.h)
                op("vector", fvy)

                for sub in range(csz // 128):
                    ti = cst // 128 + sub
                    par = ti % 2
                    dsem = d_x[par]
                    nld = dsem.bump(16)

                    def fs(sy, ti=ti, par=par, dsem=dsem, nvx_prev=v_x2.n,
                           first=(ci == 0 and sub == 0), nitf=(1 + 2 * NITER + 1)):
                        if first:
                            sy.wait_ge(v_ln.h, NTT)
                            sy.wait_ge(v_it.h, nitf)
                            sy.wait_ge(v_gt.h, NTT)
                        if nvx_prev >= 2 * NCT:
                            sy.wait_ge(v_x2.h, nvx_prev - NCT)
                        sy.dma_start(out=xbuf[:, par * D:(par + 1) * D],
                                     in_=xrh[128 * ti:128 * (ti + 1), :]).then_inc(dsem.h, 16)
                    op("sync", fs)

                    ndo_prev = (d_out[par].n if not with_attn_b else d_x2w[par].n)
                    for ct in range(NCT):
                        ps, guard, g = rotT.acquire()
                        ntr = t_tr.bump()

                        def ftt(t, ct=ct, ti=ti, ps=ps, guard=guard, nvy=nvy):
                            if ct == 0:
                                t.wait_ge(v_y.h, nvy)
                            wait_guard(t, guard)
                            t.transpose(ps[:, 0:128],
                                        y_cm[:, NT * ct + 128 * ti: NT * ct + 128 * (ti + 1)],
                                        identt[:]).then_inc(t_tr.h, 1)
                        op("tensor", ftt)

                        nvx = v_x2.bump()

                        def fvx2t(v, ct=ct, ps=ps, ntr=ntr, par=par, nld=nld, dsem=dsem,
                                  ndo_prev=ndo_prev):
                            if ct == 0:
                                v.wait_ge(dsem.h, nld)
                                if with_attn_b:
                                    if ndo_prev >= 16:
                                        v.wait_ge(d_x2w[par].h, ndo_prev)
                                else:
                                    if ndo_prev >= 16:
                                        v.wait_ge(d_out[par].h, ndo_prev)
                            v.wait_ge(t_tr.h, ntr)
                            v.tensor_add(out=x2t[:, par * D + 128 * ct: par * D + 128 * (ct + 1)],
                                         in0=xbuf[:, par * D + 128 * ct: par * D + 128 * (ct + 1)],
                                         in1=ps[:, 0:128])
                            v.mark(v_x2.h)
                        op("vector", fvx2t)
                        rotT.release(g, v_x2, nvx)

                    if with_attn_b:
                        nxw = d_x2w[par].bump(16)

                        def fgo(g, ti=ti, par=par, nvx=v_x2.n):
                            g.wait_ge(v_x2.h, nvx)
                            g.dma_start(out=x2_dram[128 * ti:128 * (ti + 1), :],
                                        in_=x2t[:, par * D:(par + 1) * D]).then_inc(d_x2w[par].h, 16)
                        op("gpsimd", fgo)
                    else:
                        dso = d_out[par]
                        nw = dso.bump(16)

                        def fgo(g, ti=ti, par=par, nvx=v_x2.n, dso=dso):
                            g.wait_ge(v_x2.h, nvx)
                            g.dma_start(out=y_out[128 * ti:128 * (ti + 1), :],
                                        in_=x2t[:, par * D:(par + 1) * D]).then_inc(dso.h, 16)
                        op("gpsimd", fgo)

        emit_final()
        NX2 = v_x2.n

        if not with_attn_b:
            def ffin(g):
                g.wait_ge(d_out[0].h, d_out[0].n)
                g.wait_ge(d_out[1].h, d_out[1].n)
            op("gpsimd", ffin)
        else:
            NX2TOT = v_x2.n
            TY_TOT = t_y.n
            nwf = d_wf.bump(32)

            def fwf(g):
                g.wait_ge(v_x2.h, NX2TOT)
                g.dma_start(out=wfix_in.rearrange("p (ct m) -> p ct m", m=1024),
                            in_=in_w.rearrange("(ct p) m -> p ct m", p=128)
                            ).then_inc(d_wf.h, 16)
                g.dma_start(out=wfix_out.rearrange("p (ct m) -> p ct m", m=1024),
                            in_=out_w.rearrange("(ct p) m -> p ct m", p=128)
                            ).then_inc(d_wf.h, 16)
            op("gpsimd", fwf)

            CH5 = [(0, 512), (512, 512), (1024, 512), (1536, 512), (2048, 128)]
            NMOD = 0
            for cj, (cst, csz) in enumerate(CH5):
                ntiles = csz // 128
                for k in range(ntiles):
                    ti5 = cst // 128 + k
                    par = (cst // 128 + k) % 2
                    dsem = d_x[par]
                    nld = dsem.bump(16)

                    def fs5(sy, ti5=ti5, par=par, dsem=dsem,
                            first=(cj == 0 and k == 0), nln_prev=v_ln.n):
                        if first:
                            sy.wait_ge(d_x2w[0].h, d_x2w[0].n)
                            sy.wait_ge(d_x2w[1].h, d_x2w[1].n)
                            sy.wait_ge(v_x2.h, NX2TOT)
                        if nln_prev >= 2:
                            sy.wait_ge(v_ln.h, nln_prev - 1)
                        sy.dma_start(out=xbuf[:, par * D:(par + 1) * D],
                                     in_=x2_dram[128 * ti5:128 * (ti5 + 1), :]
                                     ).then_inc(dsem.h, 16)
                    op("sync", fs5)

                    nst = v_st.bump()

                    def fv51(v, par=par, nld=nld, dsem=dsem):
                        v.wait_ge(dsem.h, nld)
                        xb = xbuf[:, par * D:(par + 1) * D]
                        st = stats[:, par * 12:par * 12 + 12]
                        v.bn_stats(out=st[:, 0:6], in_=xb[:, 0:512])
                        v.bn_stats(out=st[:, 6:12], in_=xb[:, 512:1024])
                        v.bn_aggr(out=mv[:, par * 2:par * 2 + 2], in_=st)
                        v.tensor_scalar(out=veps[:, par:par + 1],
                                        in0=mv[:, par * 2 + 1:par * 2 + 2],
                                        scalar1=1e-5, scalar2=None, op0=ALU.add)
                        v.mark(v_st.h)
                    op("vector", fv51)

                    nrs = s_rs.bump()

                    def fs5c(sc, par=par, nst=nst, nln_prev=v_ln.n):
                        sc.wait_ge(v_st.h, nst)
                        if nln_prev >= 1:
                            sc.wait_ge(v_ln.h, nln_prev)
                        sc.activation(out=rstd[:, par:par + 1], in_=veps[:, par:par + 1],
                                      func=AF.Ln)
                        sc.activation(out=rstd[:, par:par + 1], in_=rstd[:, par:par + 1],
                                      func=AF.Exp, scale=-0.5)
                        sc.mark(s_rs.h)
                    op("scalar", fs5c)

                    tr0 = t_tr.n
                    nln = v_ln.bump()

                    def fv52(v, par=par, nrs=nrs, tr0=tr0):
                        v.wait_ge(s_rs.h, nrs)
                        if tr0 > 0:
                            v.wait_ge(t_tr.h, tr0)
                        xb = xbuf[:, par * D:(par + 1) * D]
                        v.tensor_scalar(out=lnt[:], in0=xb,
                                        scalar1=mv[:, par * 2:par * 2 + 1],
                                        scalar2=rstd[:, par:par + 1],
                                        op0=ALU.subtract, op1=ALU.mult)
                        v.mark(v_ln.h)
                    op("vector", fv52)

                    for ct in range(NCT):
                        ps, guard, g = rotT.acquire()
                        ntr = t_tr.bump()

                        def ft5(t, ct=ct, ps=ps, guard=guard, nln=nln):
                            if ct == 0:
                                t.wait_ge(v_ln.h, nln)
                            wait_guard(t, guard)
                            t.transpose(ps[:, 0:128], lnt[:, 128 * ct:128 * (ct + 1)],
                                        identt[:]).then_inc(t_tr.h, 1)
                        op("tensor", ft5)

                        nev = v_ev.bump()

                        def fv53(v, k=k, ct=ct, ps=ps, ntr=ntr):
                            v.wait_ge(t_tr.h, ntr)
                            v.tensor_copy(out=lnchunk[:, 512 * ct + 128 * k:
                                                      512 * ct + 128 * (k + 1)],
                                          in_=ps[:, 0:128])
                            v.mark(v_ev.h)
                        op("vector", fv53)
                        rotT.release(g, v_ev, nev)
                EV5 = v_ev.n

                for cto in range(NCT):
                    psA, guard, g = rotA.acquire()
                    nmm = t_mm.bump()

                    def ftxp(t, cto=cto, psA=psA, guard=guard, csz=csz,
                             nwf=nwf, ev5=EV5, first=(cto == 0)):
                        if first:
                            t.wait_ge(d_wf.h, nwf)
                            t.wait_ge(v_ev.h, ev5)
                        wait_guard(t, guard)
                        for cti in range(NCT):
                            mi = t.matmul(psA[:, 0:csz],
                                          y_cm[:, cti * 1024 + 128 * cto:
                                               cti * 1024 + 128 * (cto + 1)],
                                          lnchunk[:, 512 * cti:512 * cti + csz],
                                          start=(cti == 0), stop=(cti == NCT - 1))
                        mi.then_inc(t_mm.h, 1)
                    op("tensor", ftxp)

                    nps = v_ps.bump()

                    def fvxp(v, cto=cto, psA=psA, nmm=nmm, csz=csz):
                        v.wait_ge(t_mm.h, nmm)
                        v.tensor_scalar(out=xproj[:, 512 * cto:512 * cto + csz],
                                        in0=psA[:, 0:csz], scalar1=inb_s[:, cto:cto + 1],
                                        scalar2=None, op0=ALU.add)
                        v.mark(v_ps.h)
                    op("vector", fvxp)
                    rotA.release(g, v_ps, nps)

                if cj == 0:
                    hm_guard = [None]
                for ht in range(32):
                    bwm = ht % 2
                    nwm = d_wm[bwm].bump(16)

                    def fgw1(g, ht=ht, bwm=bwm, prev=wslot_guard.get(bwm)):
                        if prev is not None:
                            g.wait_ge(t_mm.h, prev)
                        g.dma_start(out=wstream[:, bwm * 4096:bwm * 4096 + 1024].rearrange(
                                        "p (ct m) -> p ct m", m=128),
                                    in_=mod_w1[:, 128 * ht:128 * (ht + 1)].rearrange(
                                        "(ct p) m -> p ct m", p=128)).then_inc(d_wm[bwm].h, 16)
                    op("gpsimd", fgw1)

                    psA, guard, g = rotA.acquire()
                    nmm = t_mm.bump()
                    wslot_guard[bwm] = nmm

                    def ftm1(t, ht=ht, bwm=bwm, psA=psA, guard=guard, csz=csz, nwm=nwm):
                        t.wait_ge(d_wm[bwm].h, nwm)
                        wait_guard(t, guard)
                        for cti in range(NCT):
                            mi = t.matmul(psA[:, 0:csz],
                                          wstream[:, bwm * 4096 + 128 * cti:
                                                  bwm * 4096 + 128 * (cti + 1)],
                                          lnchunk[:, 512 * cti:512 * cti + csz],
                                          start=(cti == 0), stop=(cti == NCT - 1))
                        mi.then_inc(t_mm.h, 1)
                    op("tensor", ftm1)

                    nsh = s_h.bump()

                    def fsm1(sc, ht=ht, psA=psA, nmm=nmm, csz=csz, prev_hm=hm_guard[0]):
                        sc.wait_ge(t_mm.h, nmm)
                        if prev_hm is not None:
                            sc.wait_ge(t_mm.h, prev_hm)
                        sc.activation(out=fc1sb[:, 512 * ht:512 * ht + csz],
                                      in_=psA[:, 0:csz], func=AF.Gelu,
                                      bias=modb1_s[:, ht:ht + 1], scale=1.0)
                        sc.mark(s_h.h)
                    op("scalar", fsm1)
                    rotA.release(g, s_h, nsh)
                SH5 = s_h.n

                for cto in range(NCT):
                    bwm = cto % 2
                    nwm = d_wm[bwm].bump(16)

                    def fgw2(g, cto=cto, bwm=bwm, prev=wslot_guard.get(bwm)):
                        if prev is not None:
                            g.wait_ge(t_mm.h, prev)
                        g.dma_start(out=wstream[:, bwm * 4096:(bwm + 1) * 4096].rearrange(
                                        "p (ht m) -> p ht m", m=128),
                                    in_=mod_w2[:, 128 * cto:128 * (cto + 1)].rearrange(
                                        "(ht p) m -> p ht m", p=128)).then_inc(d_wm[bwm].h, 16)
                    op("gpsimd", fgw2)

                    psA, guard, g = rotA.acquire()
                    nmm = t_mm.bump()
                    wslot_guard[bwm] = nmm

                    def ftm2(t, cto=cto, bwm=bwm, psA=psA, guard=guard, csz=csz,
                             nwm=nwm, sh5=SH5):
                        t.wait_ge(d_wm[bwm].h, nwm)
                        if cto == 0:
                            t.wait_ge(s_h.h, sh5)
                        wait_guard(t, guard)
                        for ht in range(32):
                            mi = t.matmul(psA[:, 0:csz],
                                          wstream[:, bwm * 4096 + 128 * ht:
                                                  bwm * 4096 + 128 * (ht + 1)],
                                          fc1sb[:, 512 * ht:512 * ht + csz],
                                          start=(ht == 0), stop=(ht == 31))
                        mi.then_inc(t_mm.h, 1)
                    op("tensor", ftm2)

                    nmo = v_mo.bump()

                    def fvm2(v, cto=cto, psA=psA, nmm=nmm, csz=csz, cst=cst,
                             first=(cj == 0 and cto == 0), tyu=TY_TOT, tmu=TMM_UNITS):
                        v.wait_ge(t_mm.h, nmm)
                        if first:
                            v.wait_ge(t_mm.h, tmu)
                        v.tensor_scalar(out=mscr[:, 0:csz], in0=psA[:, 0:csz],
                                        scalar1=modb2_s[:, cto:cto + 1], scalar2=None,
                                        op0=ALU.add)
                        v.tensor_mul(out=modu[:, NT * cto + cst: NT * cto + cst + csz],
                                     in0=mscr[:, 0:csz],
                                     in1=xproj[:, 512 * cto:512 * cto + csz])
                        v.mark(v_mo.h)
                    op("vector", fvm2)
                    rotA.release(g, v_mo, nmo)
                hm_guard[0] = t_mm.n
            NMO = v_mo.n

            for ct in range(NCT):
                nd = v_diag.bump()

                def fvd(v, ct=ct, nmo=NMO, last_mm=t_mm.n, first=(ct == 0)):
                    if first:
                        v.wait_ge(t_mm.h, last_mm)
                    for j in range(64):
                        v.tensor_scalar(out=diag64[:, 128 * j:128 * (j + 1)],
                                        in0=identt[:], scalar1=kern_s[:, 64 * ct + j:64 * ct + j + 1],
                                        scalar2=None, op0=ALU.mult)
                    v.mark(v_diag.h)
                op("vector", fvd)

                for c4 in range(4):
                    psC, guardC, gC = rotC.acquire()
                    ncv = t_cv.bump()

                    def ftc(t, ct=ct, c4=c4, psC=psC, guardC=guardC, nd=nd):
                        t.wait_ge(v_diag.h, nd)
                        wait_guard(t, guardC)
                        base = NT * ct + 128 + 512 * c4
                        for j in range(64):
                            mi = t.matmul(psC[:, 0:512],
                                          diag64[:, 128 * j:128 * (j + 1)],
                                          modu[:, base - j: base - j + 512],
                                          start=(j == 0), stop=(j == 63))
                        mi.then_inc(t_cv.h, 1)
                    op("tensor", ftc)

                    ncs = v_cs.bump()

                    def fvc(v, ct=ct, c4=c4, psC=psC, ncv=ncv):
                        v.wait_ge(t_cv.h, ncv)
                        v.tensor_copy(out=conv_full[:, 2048 * ct + 512 * c4:
                                                    2048 * ct + 512 * (c4 + 1)],
                                      in_=psC[:, 0:512])
                        v.mark(v_cs.h)
                    op("vector", fvc)
                    rotC.release(gC, v_cs, ncs)

            for c4 in range(4):
                for cto in range(NCT):
                    psA, guard, g = rotA.acquire()
                    nmm = t_mm.bump()

                    def fto(t, c4=c4, cto=cto, psA=psA, guard=guard,
                            needcs=(7 * 4 + c4 + 1)):
                        t.wait_ge(v_cs.h, needcs)
                        wait_guard(t, guard)
                        for cti in range(NCT):
                            mi = t.matmul(psA[:, 0:512],
                                          y_cm[:, 8192 + cti * 1024 + 128 * cto:
                                               8192 + cti * 1024 + 128 * (cto + 1)],
                                          conv_full[:, 2048 * cti + 512 * c4:
                                                    2048 * cti + 512 * (c4 + 1)],
                                          start=(cti == 0), stop=(cti == NCT - 1))
                        mi.then_inc(t_mm.h, 1)
                    op("tensor", fto)

                    nps = v_ps.bump()

                    def fvo(v, cto=cto, psA=psA, nmm=nmm):
                        v.wait_ge(t_mm.h, nmm)
                        v.tensor_scalar(out=osb[:, 512 * cto:512 * (cto + 1)],
                                        in0=psA[:, 0:512], scalar1=outb_s[:, cto:cto + 1],
                                        scalar2=None, op0=ALU.add)
                        v.mark(v_ps.h)
                    op("vector", fvo)
                    rotA.release(g, v_ps, nps)
                OSB_PS = v_ps.n

                for st4 in range(4):
                    ti_own = 4 * c4 + st4
                    nt_tile = 1 + ti_own
                    par = ti_own % 2
                    dsem = d_x[par]
                    nld = dsem.bump(16)

                    def fsx3(sy, nt_tile=nt_tile, par=par, dsem=dsem, nln_prev=v_ln.n,
                             nx3_prev=v_x3.n):
                        sy.wait_ge(v_ln.h, nln_prev)
                        if nx3_prev >= 2:
                            sy.wait_ge(v_x3.h, nx3_prev - 1)
                        sy.dma_start(out=xbuf[:, par * D:(par + 1) * D],
                                     in_=x2_dram[128 * nt_tile:128 * (nt_tile + 1), :]
                                     ).then_inc(dsem.h, 16)
                    op("sync", fsx3)

                    for ct in range(NCT):
                        ps, guard, g = rotT.acquire()
                        ntr = t_tr.bump()

                        def ftt3(t, ct=ct, st4=st4, ps=ps, guard=guard, ops_n=OSB_PS):
                            if ct == 0:
                                t.wait_ge(v_ps.h, ops_n)
                            wait_guard(t, guard)
                            t.transpose(ps[:, 0:128],
                                        osb[:, 512 * ct + 128 * st4:512 * ct + 128 * (st4 + 1)],
                                        identt[:]).then_inc(t_tr.h, 1)
                        op("tensor", ftt3)

                        nx3 = v_x3.bump()

                        def fvx3(v, ct=ct, ps=ps, ntr=ntr, par=par, nld=nld, dsem=dsem,
                                 ndo_prev=d_out[par].n):
                            if ct == 0:
                                v.wait_ge(dsem.h, nld)
                                if ndo_prev >= 16:
                                    v.wait_ge(d_out[par].h, ndo_prev)
                            v.wait_ge(t_tr.h, ntr)
                            v.tensor_add(out=x3t[:, par * D + 128 * ct:par * D + 128 * (ct + 1)],
                                         in0=xbuf[:, par * D + 128 * ct:par * D + 128 * (ct + 1)],
                                         in1=ps[:, 0:128])
                            v.mark(v_x3.h)
                        op("vector", fvx3)
                        rotT.release(g, v_x3, nx3)
                    NX3 = v_x3.n

                    ndo = d_out[par].bump(16)

                    def fgx3(g, ti_own=ti_own, par=par, nx3=NX3):
                        g.wait_ge(v_x3.h, nx3)
                        g.dma_start(out=x3_out[128 * ti_own:128 * (ti_own + 1), :],
                                    in_=x3t[:, par * D:(par + 1) * D]).then_inc(d_out[par].h, 16)
                    op("gpsimd", fgx3)

                    nst = v_st.bump()

                    def fvl3(v, par=par, nx3=NX3):
                        xb = x3t[:, par * D:(par + 1) * D]
                        st = stats[:, par * 12:par * 12 + 12]
                        v.bn_stats(out=st[:, 0:6], in_=xb[:, 0:512])
                        v.bn_stats(out=st[:, 6:12], in_=xb[:, 512:1024])
                        v.bn_aggr(out=mv[:, par * 2:par * 2 + 2], in_=st)
                        v.tensor_scalar(out=veps[:, par:par + 1],
                                        in0=mv[:, par * 2 + 1:par * 2 + 2],
                                        scalar1=1e-5, scalar2=None, op0=ALU.add)
                        v.mark(v_st.h)
                    op("vector", fvl3)

                    nrs = s_rs.bump()

                    def fsl3(sc, par=par, nst=nst, nln_prev=v_ln.n):
                        sc.wait_ge(v_st.h, nst)
                        if nln_prev >= 1:
                            sc.wait_ge(v_ln.h, nln_prev)
                        sc.activation(out=rstd[:, par:par + 1], in_=veps[:, par:par + 1],
                                      func=AF.Ln)
                        sc.activation(out=rstd[:, par:par + 1], in_=rstd[:, par:par + 1],
                                      func=AF.Exp, scale=-0.5)
                        sc.mark(s_rs.h)
                    op("scalar", fsl3)

                    tr0 = t_tr.n
                    nln = v_ln.bump()

                    def fvn3(v, par=par, nrs=nrs, tr0=tr0):
                        v.wait_ge(s_rs.h, nrs)
                        v.wait_ge(t_tr.h, tr0)
                        xb = x3t[:, par * D:(par + 1) * D]
                        v.tensor_scalar(out=lnt[:], in0=xb,
                                        scalar1=mv[:, par * 2:par * 2 + 1],
                                        scalar2=rstd[:, par:par + 1],
                                        op0=ALU.subtract, op1=ALU.mult)
                        v.mark(v_ln.h)
                    op("vector", fvn3)

                    for ct in range(NCT):
                        ps, guard, g = rotT.acquire()
                        ntr = t_tr.bump()

                        def ftz(t, ct=ct, ps=ps, guard=guard, nln=nln):
                            if ct == 0:
                                t.wait_ge(v_ln.h, nln)
                            wait_guard(t, guard)
                            t.transpose(ps[:, 0:128], lnt[:, 128 * ct:128 * (ct + 1)],
                                        identt[:]).then_inc(t_tr.h, 1)
                        op("tensor", ftz)

                        nev = v_ev.bump()

                        def fvze(v, ct=ct, ps=ps, ntr=ntr):
                            v.wait_ge(t_tr.h, ntr)
                            v.tensor_copy(out=lnzT[:, 128 * ct:128 * (ct + 1)], in_=ps[:, 0:128])
                            v.mark(v_ev.h)
                        op("vector", fvze)
                        rotT.release(g, v_ev, nev)
                    EVZ = v_ev.n

                    ntz = t_z.bump()

                    def ftzm(t, evz=EVZ, nz_prev=v_z.n):
                        t.wait_ge(v_ev.h, evz)
                        if nz_prev >= 1:
                            t.wait_ge(v_z.h, nz_prev)
                        for ct in range(NCT):
                            mi = t.matmul(psY[2][:, 0:8], lnzT[:, 128 * ct:128 * (ct + 1)],
                                          projT_s[:, 8 * ct:8 * (ct + 1)],
                                          start=(ct == 0), stop=(ct == NCT - 1))
                        mi.then_inc(t_z.h, 1)
                    op("tensor", ftzm)

                    nvz = v_z.bump()

                    def fvzz(v, ti_own=ti_own, ntz=ntz):
                        v.wait_ge(t_z.h, ntz)
                        v.tensor_copy(out=z_sb[:, 8 * ti_own:8 * (ti_own + 1)],
                                      in_=psY[2][:, 0:8])
                        v.mark(v_z.h)
                    op("vector", fvzz)

            nzo = d_zo.bump(16)

            def fgz(g, nvz_tot=v_z.n):
                g.wait_ge(v_z.h, nvz_tot)
                g.dma_start(out=z3_out.rearrange("(q p) e -> p q e", p=128),
                            in_=z_sb[:].rearrange("p (q e) -> p q e", e=8)).then_inc(d_zo.h, 16)
                g.wait_ge(d_out[0].h, d_out[0].n)
                g.wait_ge(d_out[1].h, d_out[1].n)
                g.wait_ge(d_zo.h, 16)
            op("gpsimd", fgz)

        with nc.Block() as block:
            @block.sync
            def _(e):
                for f in OPS["sync"]:
                    f(e)

            @block.gpsimd
            def _(e):
                for f in OPS["gpsimd"]:
                    f(e)

            @block.tensor
            def _(e):
                for f in OPS["tensor"]:
                    f(e)

            @block.vector
            def _(e):
                ce = CE(e, chV)
                for f in OPS["vector"]:
                    f(ce)

            @block.scalar
            def _(e):
                ce = CE(e, chS)
                for f in OPS["scalar"]:
                    f(ce)

    return nc


def emit_attn_b(L):
    raise NotImplementedError


def prep_ffn_inputs(with_attn_b, x_res, z_raw, proj_b, expert_bias,
                    fc1_w, fc1_b, fc2_w, fc2_b, sh1_w, sh1_b, sh2_w, sh2_b,
                    attnb=None):
    NH = 128 if with_attn_b else 0
    f32 = np.float32
    proj_b = np.asarray(proj_b, f32).reshape(8)
    expert_bias = np.asarray(expert_bias, f32).reshape(8)
    pb8 = np.broadcast_to(proj_b[None, :], (128, 8)).copy()
    eb8 = np.broadcast_to(expert_bias[None, :], (128, 8)).copy()
    u1 = np.zeros((NU, D, D), f32)
    u2 = np.zeros((NU, D, D), f32)
    ub = np.zeros((128, NU * 8), f32)
    for ui, unit in enumerate(UNITS):
        if unit[0] == "sh":
            q = unit[1]
            u1[ui] = sh1_w[1024 * q:1024 * (q + 1), :].T
            u2[ui] = sh2_w[:, 1024 * q:1024 * (q + 1)].T
            bb = sh1_b[1024 * q:1024 * (q + 1)]
        else:
            e, h = unit[1], unit[2]
            u1[ui] = fc1_w[e, 1024 * h:1024 * (h + 1), :].T
            u2[ui] = fc2_w[e][:, 1024 * h:1024 * (h + 1)].T
            bb = fc1_b[e, 1024 * h:1024 * (h + 1)]
        ub[:, ui * 8:(ui + 1) * 8] = bb.reshape(8, 128).T
    fc2bT_h = np.asarray(fc2_b, f32)
    sh2b_row = np.asarray(sh2_b, f32).reshape(1, D)

    def cm(v, nct):
        return np.ascontiguousarray(np.asarray(v, f32).reshape(nct, 128).T)

    maps = []
    for c in range(8):
        b, p = c // 2, c % 2
        own = x_res[b, 2048 * p:2048 * (p + 1)]
        m = {}
        if with_attn_b:
            start = 2048 * p
            idx = (np.arange(start - 128, start) % 4096)
            halo = x_res[b][idx]
            m["xrh"] = np.ascontiguousarray(np.concatenate([halo, own], 0), f32)
            zh = z_raw[b][idx]
            zown = z_raw[b, 2048 * p:2048 * (p + 1)]
            mask = np.ones((4, 4096), bool)
            mask[b, idx] = False
            mask[b, 2048 * p:2048 * (p + 1)] = False
            zrest = z_raw[mask.reshape(4, 4096)]
            za = np.concatenate([zh, zown, zrest], 0)
            m["zall"] = np.ascontiguousarray(
                za.reshape(128, 128, 8).transpose(1, 0, 2).reshape(128, 1024), f32)
        else:
            m["xrh"] = np.ascontiguousarray(own, f32)
            mask = np.ones((4, 4096), bool)
            mask[b, 2048 * p:2048 * (p + 1)] = False
            zrest = z_raw[mask]
            za = np.concatenate([z_raw[b, 2048 * p:2048 * (p + 1)], zrest], 0)
            m["zall"] = np.ascontiguousarray(
                za.reshape(128, 128, 8).transpose(1, 0, 2).reshape(128, 1024), f32)
        m["pb8"] = pb8
        m["eb8"] = eb8
        m["u1w"] = u1
        m["u2w"] = u2
        m["ubias"] = ub
        m["fc2bT"] = fc2bT_h
        m["sh2b_row"] = sh2b_row
        if with_attn_b:
            a = attnb
            m["in_w"] = np.asarray(a["in_w"], f32)
            m["in_b_cm"] = cm(a["in_b"], 8)
            m["mod_w1"] = np.asarray(a["mod_w1"], f32)
            m["mod_b1_cm"] = cm(a["mod_b1"], 32)
            m["mod_w2"] = np.asarray(a["mod_w2"], f32)
            m["mod_b2_cm"] = cm(a["mod_b2"], 8)
            kern = np.asarray(a["kernel"], f32)
            m["kern_cm"] = np.ascontiguousarray(
                kern.reshape(8, 128, 64).transpose(1, 0, 2).reshape(128, 512))
            m["out_w"] = np.asarray(a["out_w"], f32)
            m["out_b_cm"] = cm(a["out_b"], 8)
            pT = np.asarray(a["proj_w"], f32).T
            m["projT"] = np.ascontiguousarray(
                pT.reshape(8, 128, 8).transpose(1, 0, 2).reshape(128, 64))
        maps.append(m)
    return maps


_NC_CACHE = {}


def _get_nc(which):
    if which not in _NC_CACHE:
        if which == "l1":
            _NC_CACHE[which] = build_launch1()
        elif which == "l2":
            _NC_CACHE[which] = build_ffn_launch(True)
        else:
            _NC_CACHE[which] = build_ffn_launch(False)
    return _NC_CACHE[which]


def kernel(**inputs):
    from concourse.bass_utils import run_bass_kernel_spmd
    f32 = np.float32
    g = lambda k: np.asarray(inputs[k], f32)
    x = g("L0_attn_a_x")
    dlog = float(np.asarray(inputs["L0_attn_a_decay_logit"]))
    osc = float(np.asarray(inputs["L0_attn_a_out_scale"]))
    qs = g("L0_attn_a_q_scale")
    ks = g("L0_attn_a_k_scale")
    proj_w = g("L0_ffn_proj_w")
    proj_b = g("L0_ffn_proj_b")
    ebias = g("L0_ffn_expert_bias")
    fc1_w = g("L0_ffn_fc1_w")
    fc1_b = g("L0_ffn_fc1_b")
    fc2_w = g("L0_ffn_fc2_w")
    fc2_b = g("L0_ffn_fc2_b")
    sh1_w = g("L0_ffn_sh1_w")
    sh1_b = g("L0_ffn_sh1_b")
    sh2_w = g("L0_ffn_sh2_w")
    sh2_b = g("L0_ffn_sh2_b")
    attnb = {
        "in_w": g("L0_attn_b_in_w"), "in_b": g("L0_attn_b_in_b"),
        "mod_w1": g("L0_attn_b_mod_w1"), "mod_b1": g("L0_attn_b_mod_b1"),
        "mod_w2": g("L0_attn_b_mod_w2"), "mod_b2": g("L0_attn_b_mod_b2"),
        "kernel": g("L0_attn_b_kernel"), "out_w": g("L0_attn_b_out_w"),
        "out_b": g("L0_attn_b_out_b"), "proj_w": proj_w,
    }
    cores = list(range(8))

    nc1 = _get_nc("l1")
    maps1 = prep_launch1_inputs(x, qs, ks, dlog, osc, proj_w)
    r1 = run_bass_kernel_spmd(nc1, maps1, core_ids=cores)
    x1 = np.zeros((4, 4096, 1024), f32)
    z1 = np.zeros((4, 4096, 8), f32)
    for c in range(8):
        b, p = c // 2, c % 2
        x1[b, 2048 * p:2048 * (p + 1)] = r1.results[c]["x1_out"]
        z1[b, 2048 * p:2048 * (p + 1)] = r1.results[c]["z1_out"]

    nc2 = _get_nc("l2")
    maps2 = prep_ffn_inputs(True, x1, z1, proj_b, ebias, fc1_w, fc1_b, fc2_w, fc2_b,
                            sh1_w, sh1_b, sh2_w, sh2_b, attnb=attnb)
    r2 = run_bass_kernel_spmd(nc2, maps2, core_ids=cores)
    x3 = np.zeros((4, 4096, 1024), f32)
    z3 = np.zeros((4, 4096, 8), f32)
    for c in range(8):
        b, p = c // 2, c % 2
        x3[b, 2048 * p:2048 * (p + 1)] = r2.results[c]["x3_out"]
        z3[b, 2048 * p:2048 * (p + 1)] = r2.results[c]["z3_out"]

    nc3 = _get_nc("l3")
    maps3 = prep_ffn_inputs(False, x3, z3, proj_b, ebias, fc1_w, fc1_b, fc2_w, fc2_b,
                            sh1_w, sh1_b, sh2_w, sh2_b)
    r3 = run_bass_kernel_spmd(nc3, maps3, core_ids=cores)
    y = np.zeros((4, 4096, 1024), f32)
    for c in range(8):
        b, p = c // 2, c % 2
        y[b, 2048 * p:2048 * (p + 1)] = r3.results[c]["y_out"]
    return y
